# version 02
# baseline (speedup 1.0000x reference)
import contextlib
import numpy as np
import ml_dtypes
import concourse.bass as bass
import concourse.mybir as mybir
from concourse.bass_core_rust import IndirectOffsetOnAxis
from concourse.bass_utils import run_bass_kernel_spmd

F32 = mybir.dt.float32
BF16 = mybir.dt.bfloat16
I32 = mybir.dt.int32
U32 = mybir.dt.uint32
ALU = mybir.AluOpType
ACT = mybir.ActivationFunctionType
AX = mybir.AxisListType.X

NCORE = 8
TLOC = 2048
NT = 16
D = 1024
DC = 8
E = 32
EL = 4
CAP = 1024
SPP = CAP // 128
IFF = 512
NH = 8
HD = 256
LR = 64
BF2 = np.dtype(ml_dtypes.bfloat16)
RG_ALL = [[0, 1, 2, 3, 4, 5, 6, 7]]
RG_BATCH = [[0, 1, 2, 3], [4, 5, 6, 7]]


def _ap_free(ap):
    try:
        sh = list(ap.shape)
        free = 1
        for d in sh[1:]:
            free *= int(d)
        return free
    except Exception:
        return 1 << 20


class VE:

    COUNTED = {}

    def __init__(self, ve, s_obj):
        self._ve = ve
        self._s = s_obj

    def __getattr__(self, name):
        f = getattr(self._ve, name)
        if name in ("wait_ge", "engine_nop", "dma_start", "nop"):
            return f

        def wrap(*args, **kwargs):
            inst = f(*args, **kwargs)
            out = kwargs.get("out", kwargs.get("ap", args[0] if args else None))
            dur = _ap_free(out)
            for k in ("in_", "in0", "in1", "in_values", "data0"):
                if k in kwargs:
                    dur = max(dur, _ap_free(kwargs[k]))
            if len(args) >= 2:
                dur = max(dur, _ap_free(args[1]))
            self._s.inc(inst, 1)
            VE.COUNTED[id(inst)] = inst
            if dur < 256:
                self._ve.wait_ge(self._s.h, self._s.v)
                self._s.inc(self._ve.engine_nop(), 1)
            return inst

        return wrap


class S:
    def __init__(self, nc, name):
        self.h = nc.alloc_semaphore(name)
        self.v = 0

    def inc(self, inst, k=1):
        if id(inst) in VE.COUNTED:
            return inst
        inst.then_inc(self.h, k)
        self.v += k
        return inst

    def wait(self, eng, v=None):
        eng.wait_ge(self.h, self.v if v is None else v)


def build(debug_taps=(), n_ffn=2, do_attn=True, stop_after=None):
    VE.COUNTED.clear()
    nc = bass.Bass("TRN2", target_bir_lowering=False, num_devices=NCORE, detect_race_conditions=False)

    def din(name, shape, dt):
        return nc.dram_tensor(name, shape, dt, kind="ExternalInput").ap()

    x_in = din("x", [TLOC, D], F32)
    wqT_in = din("wqT", [D, NH * HD], F32)
    wkvdT_in = din("wkvdT", [D, LR], F32)
    wkupT_in = din("wkupT", [LR, NH * HD], F32)
    wvupT_in = din("wvupT", [LR, NH * HD], F32)
    woT_in = din("woT", [NH * HD, D], F32)
    gwT_in = din("gwT", [D, E], BF16)
    bias_in = din("bias", [E], F32)
    w1_in = din("w1", [EL, D, IFF], BF16)
    w3_in = din("w3", [EL, D, IFF], BF16)
    w2_in = din("w2", [EL, IFF, D], BF16)
    sw1T_in = din("sw1T", [D, IFF], BF16)
    sw3T_in = din("sw3T", [D, IFF], BF16)
    sw2T_in = din("sw2T", [IFF, D], BF16)
    eids_in = din("eids", [128, EL], F32)
    tokoff_in = din("tokoff", [128, 1], F32)
    premask_in = din("premask", [NCORE, 1], F32)

    out_t = nc.dram_tensor("out", [TLOC, D], F32, kind="ExternalOutput").ap()

    xn_dram = nc.dram_tensor("xn_dram", [TLOC, D], BF16, kind="Internal")
    x_all = nc.dram_tensor("x_all", [NCORE * TLOC, D], BF16, kind="Internal")
    cnt_loc = nc.dram_tensor("cnt_loc", [E], F32, kind="Internal")
    cnt_all = nc.dram_tensor("cnt_all", [NCORE, E], F32, kind="Internal")
    sp_loc = nc.dram_tensor("sp_loc", [E * (CAP + 1), 2], F32, kind="Internal")
    sp_all = nc.dram_tensor("sp_all", [E * (CAP + 1), 2], F32, kind="Internal")
    xe_dram = nc.dram_tensor("xe_dram", [CAP, D], BF16, kind="Internal")
    partial = nc.dram_tensor("partial", [NCORE * TLOC, D], BF16, kind="Internal")
    moe_dram = nc.dram_tensor("moe_dram", [TLOC, D], BF16, kind="Internal")
    zeros_dram = nc.dram_tensor("zeros_dram", [TLOC, D], BF16, kind="Internal")
    wq_bf = nc.dram_tensor("wq_bf", [D, NH * HD], BF16, kind="Internal")
    woT_bf = nc.dram_tensor("woT_bf", [NH * HD, D], BF16, kind="Internal")
    lat_in_d = nc.dram_tensor("lat_in_d", [LR, TLOC], BF16, kind="Internal")
    lat_all_d = nc.dram_tensor("lat_all_d", [4 * LR, TLOC], BF16, kind="Internal")
    ao_dram = nc.dram_tensor("ao_dram", [TLOC, NH * HD], BF16, kind="Internal")
    q_dram = nc.dram_tensor("q_dram", [NH * HD, TLOC], BF16, kind="Internal")

    ident_bf_c = nc.inline_tensor(np.eye(128, dtype=BF2), name="ident_bf_c")
    ident_f_c = nc.inline_tensor(np.eye(128, dtype=np.float32), name="ident_f_c")
    iota32_cc = nc.inline_tensor(
        np.tile(np.arange(E, dtype=np.float32), (128, 1)), name="iota32_cc")
    locid_cc = nc.inline_tensor(
        (np.arange(NT)[None, :] * 128 + np.arange(128)[:, None]).astype(np.float32),
        name="locid_cc")
    neg1_cc = nc.inline_tensor(np.full((128, 1024), -1.0, np.float32), name="neg1_cc")
    p32_cc = nc.inline_tensor((np.arange(128)[:, None] * SPP).astype(np.float32),
                              name="p32_cc")

    taps = {}

    def tap_decl(name, shape, dt=F32):
        if name in debug_taps:
            taps[name] = nc.dram_tensor("tap_" + name, shape, dt,
                                        kind="ExternalOutput").ap()

    tap_decl("xn", [TLOC, D])
    tap_decl("lnstats", [128, NT * 4])
    tap_decl("scoresT", [E, TLOC])
    tap_decl("top8", [128, NT * 8])
    tap_decl("idx8", [128, NT * 8])
    tap_decl("selT", [E, TLOC])
    tap_decl("gposT", [E, TLOC])
    tap_decl("incl", [E, TLOC])
    tap_decl("cnta", [NCORE, E])
    tap_decl("pref", [E, 1])
    tap_decl("ones", [E, TLOC])
    tap_decl("gates", [128, NT * 4])
    tap_decl("spall", [E * (CAP + 1), 2])
    tap_decl("xres1", [TLOC, D])
    tap_decl("moe", [TLOC, D])
    tap_decl("lat", [4 * LR, TLOC])
    tap_decl("xres2", [TLOC, D])

    ctx = contextlib.ExitStack()
    with ctx:
        sIn = S(nc, "sIn")
        sSt = S(nc, "sSt")
        sXb = S(nc, "sXb")
        sGd = S(nc, "sGd")
        sGi = S(nc, "sGi")
        sCc = S(nc, "sCc")
        sPE = S(nc, "sPE")
        sV = S(nc, "sV")
        sSc = S(nc, "sSc")
        ve = VE(nc.vector, sV)

        sy, gp, pe, sc = nc.sync, nc.gpsimd, nc.tensor, nc.scalar



        def sb(name, shape, dt):
            return ctx.enter_context(nc.sbuf_tensor(name, shape, dt))

        x_res = sb("x_res", [128, NT, D], F32)
        identb = sb("identb", [128, 128], BF16)
        identf = sb("identf", [128, 128], F32)
        iota32 = sb("iota32", [128, E], F32)
        locids = sb("locids", [128, NT], F32)
        tokids = sb("tokids", [128, NT], F32)
        p32t = sb("p32t", [128, 1], F32)
        neg1t = sb("neg1t", [128, 1024], F32)
        zero_bf = sb("zero_bf", [128, 1024], BF16)
        zf32 = sb("zf32", [128, E], F32)
        gw_s = sb("gw_s", [128, DC, E], BF16)
        sw1_s = sb("sw1_s", [128, DC, IFF], BF16)
        sw3_s = sb("sw3_s", [128, DC, IFF], BF16)
        sw2_s = sb("sw2_s", [128, 4, D], BF16)
        bias_col = sb("bias_col", [E, 1], F32)
        bias_row = sb("bias_row", [1, E], F32)
        bias_bc = sb("bias_bc", [128, E], F32)
        premask_s = sb("premask_s", [NCORE, 1], F32)
        eids_s = sb("eids_s", [128, EL], F32)
        tokoff_s = sb("tokoff_s", [128, 1], F32)
        m30 = sb("m30", [128, 1], F32)
        ones_1 = sb("ones_1", [1, 128], F32)
        czero = sb("czero", [128, 1], F32)
        cone = sb("cone", [128, 1], F32)
        wkvd_s = sb("wkvd_s", [128, DC, LR], BF16)
        wkup_s = sb("wkup_s", [LR, NH * HD], BF16)
        wvup_s = sb("wvup_s", [LR, NH * HD], BF16)
        def dma_in(out_ap, in_ap):
            sIn.inc(sy.dma_start(out=out_ap, in_=in_ap), 16)

        dma_in(x_res[:], x_in.rearrange("(j p) d -> p j d", p=128))
        dma_in(identb[:], ident_bf_c.ap())
        dma_in(identf[:], ident_f_c.ap())
        dma_in(iota32[:], iota32_cc.ap())
        dma_in(locids[:], locid_cc.ap())
        dma_in(p32t[:], p32_cc.ap())
        dma_in(neg1t[:], neg1_cc.ap())
        dma_in(gw_s[:], gwT_in.rearrange("(c p) e -> p c e", p=128))
        dma_in(sw1_s[:], sw1T_in.rearrange("(c p) i -> p c i", p=128))
        dma_in(sw3_s[:], sw3T_in.rearrange("(c p) i -> p c i", p=128))
        dma_in(sw2_s[:], sw2T_in.rearrange("(c p) d -> p c d", p=128))
        dma_in(bias_col[:], bias_in.rearrange("(e one) -> e one", one=1))
        dma_in(bias_row[:], bias_in.rearrange("(one e) -> one e", one=1))
        dma_in(premask_s[:], premask_in)
        dma_in(eids_s[:], eids_in)
        dma_in(tokoff_s[:], tokoff_in)

        sIn.wait(ve)
        sV.inc(ve.memset(zero_bf[:], 0.0))
        sV.inc(ve.memset(zf32[:], 0.0))
        sV.inc(ve.memset(m30[:], -30.0))
        sV.inc(ve.memset(ones_1[:], 1.0))
        sV.inc(ve.memset(czero[:], 0.0))
        sV.inc(ve.memset(cone[:], 1.0))
        nc.const_aps.register(F32, 0.0, czero[:])
        nc.const_aps.register(F32, 1.0, cone[:])
        sV.inc(ve.tensor_scalar_add(tokids[:], locids[:], tokoff_s[:]))

        sV.wait(sy)
        for j in range(NT):
            sSt.inc(sy.dma_start(out=zeros_dram.ap()[j * 128:(j + 1) * 128, :],
                                 in_=zero_bf[:]), 16)
        zeros_done_v = sSt.v

        pctx = contextlib.ExitStack()
        stage_f = pctx.enter_context(nc.sbuf_tensor("stage_f", [128, 2048], F32))
        stage_b = pctx.enter_context(nc.sbuf_tensor("stage_b", [128, 2048], BF16))

        def cast_weight(src_ap, nrows, ncols, dst_dram=None, dst_sbuf=None):
            for cc in range(nrows // 128):
                rows = slice(cc * 128, (cc + 1) * 128)
                sV.wait(sy)
                sIn.inc(sy.dma_start(out=stage_f[:, 0:ncols], in_=src_ap[rows, :]), 16)
                sIn.wait(ve)
                sSt.wait(ve)
                if dst_sbuf is not None:
                    sV.inc(ve.tensor_copy(out=dst_sbuf[cc], in_=stage_f[:, 0:ncols]))
                else:
                    sV.inc(ve.tensor_copy(out=stage_b[:, 0:ncols],
                                          in_=stage_f[:, 0:ncols]))
                    sV.wait(sy)
                    sSt.inc(sy.dma_start(out=dst_dram.ap()[rows, :],
                                         in_=stage_b[:, 0:ncols]), 16)

        if do_attn:
            cast_weight(wqT_in, D, NH * HD, dst_dram=wq_bf)
            cast_weight(woT_in, NH * HD, D, dst_dram=woT_bf)
            cast_weight(wkvdT_in, D, LR,
                        dst_sbuf=[wkvd_s[:, c, :] for c in range(DC)])
            sV.wait(sy)
            sIn.inc(sy.dma_start(out=stage_f[0:LR, :], in_=wkupT_in), 16)
            sIn.wait(ve)
            sV.inc(ve.tensor_copy(out=wkup_s[:], in_=stage_f[0:LR, :]))
            sV.wait(sy)
            sIn.inc(sy.dma_start(out=stage_f[0:LR, :], in_=wvupT_in), 16)
            sIn.wait(ve)
            sV.inc(ve.tensor_copy(out=wvup_s[:], in_=stage_f[0:LR, :]))

        pctx.close()

        with nc.psum_tensor("ps_bb", [128, 512], F32) as ps_bb:
            sV.wait(pe)
            sIn.wait(pe)
            sPE.inc(pe.matmul(ps_bb[:, 0:E], lhsT=ones_1[:], rhs=bias_row[:],
                              start=True, stop=True))
            sPE.wait(ve)
            sV.inc(ve.tensor_copy(out=bias_bc[:], in_=ps_bb[:, 0:E]))

        def tap_sbuf(name, src_ap):
            if name not in taps:
                return
            sV.wait(gp)
            sSc.wait(gp)
            sSt.inc(gp.dma_start(out=taps[name], in_=src_ap), 16)

        def tap_dram(name, src_dram_ap):
            if name not in taps:
                return
            sCc.wait(gp)
            sGi.wait(gp)
            sSt.wait(gp)
            sGd.inc(gp.dma_start(out=taps[name], in_=src_dram_ap), 16)

        def emit_ln_to_fm(xn_fm, lntag):
            with contextlib.ExitStack() as lctx:
                def lsb(nm, shape, dt):
                    return lctx.enter_context(nc.sbuf_tensor(nm + lntag, shape, dt))
                sPE.wait(sc)
                sPE.wait(ve)
                sPE.wait(sy)
                xnst = lsb("xnst", [128, 2, D], BF16)
                x2j = lsb("x2j", [128, D], BF16)
                s1c = lsb("s1c", [128, NT], F32)
                s2c = lsb("s2c", [128, NT], F32)
                mc_ = lsb("mc_", [128, NT], F32)
                varc = lsb("varc", [128, NT], F32)
                rstdc = lsb("rstdc", [128, NT], F32)
                ttc = lsb("ttc", [128, NT], F32)
                lnbc = lsb("lnbc", [128, NT], F32)
                for j in range(NT):
                    xj = x_res[:, j, :]
                    sV.inc(ve.reduce_sum(out=s1c[:, j:j + 1], in_=xj, axis=AX))
                    sSc.inc(sc.activation(out=x2j[:], in_=xj, func=ACT.Square,
                                          bias=0.0, scale=1.0,
                                          accum_out=s2c[:, j:j + 1]))
                sSc.wait(ve)
                sV.inc(ve.tensor_scalar_mul(mc_[:], s1c[:], 1.0 / D))
                sV.inc(ve.tensor_scalar_mul(varc[:], s2c[:], 1.0 / D))
                sV.inc(ve.tensor_mul(ttc[:], mc_[:], mc_[:]))
                sV.inc(ve.tensor_sub(varc[:], varc[:], ttc[:]))
                sV.inc(ve.tensor_scalar_add(varc[:], varc[:], 1e-5))
                sV.wait(sc)
                sSc.inc(sc.activation(out=ttc[:], in_=varc[:], func=ACT.Ln,
                                      bias=0.0, scale=1.0))
                sSc.inc(sc.activation(out=rstdc[:], in_=ttc[:], func=ACT.Exp,
                                      bias=0.0, scale=-0.5))
                sSc.wait(ve)
                sV.inc(ve.tensor_mul(ttc[:], rstdc[:], rstdc[:]))
                sV.inc(ve.tensor_mul(ttc[:], ttc[:], varc[:]))
                sV.inc(ve.tensor_scalar(ttc[:], ttc[:], -0.5, 1.5,
                                        op0=ALU.mult, op1=ALU.add))
                sV.inc(ve.tensor_mul(rstdc[:], rstdc[:], ttc[:]))
                sV.inc(ve.scalar_tensor_tensor(out=lnbc[:], in0=mc_[:], scalar=-1.0,
                                               in1=rstdc[:], op0=ALU.mult,
                                               op1=ALU.mult))
                sV.wait(sc)
                st_v = {}
                for j in range(NT):
                    par = j % 2
                    if j >= 2:
                        sSt.wait(sc, st_v[j - 2])
                    sSc.inc(sc.activation(out=xnst[:, par, :], in_=x_res[:, j, :],
                                          func=ACT.Identity,
                                          bias=lnbc[:, j:j + 1],
                                          scale=rstdc[:, j:j + 1]))
                    sSc.wait(sy)
                    sSt.inc(sy.dma_start(out=xn_dram.ap()[j * 128:(j + 1) * 128, :],
                                         in_=xnst[:, par, :]), 16)
                    st_v[j] = sSt.v
                if "lnstats" in taps:
                    sV.wait(sy)
                    sSc.wait(sy)
                    tl = taps["lnstats"].rearrange("p (k j) -> p k j", k=4)
                    sSt.inc(sy.dma_start(out=tl[:, 0, :], in_=varc[:]), 16)
                    sSt.inc(sy.dma_start(out=tl[:, 1, :], in_=ttc[:]), 16)
                    sSt.inc(sy.dma_start(out=tl[:, 2, :], in_=rstdc[:]), 16)
                    sSt.inc(sy.dma_start(out=tl[:, 3, :], in_=lnbc[:]), 16)
                sSt.wait(sy)
                for c in range(DC):
                    sXb.inc(sy.dma_start_transpose(
                        out=xn_fm[:, c, :],
                        in_=xn_dram.ap()[:, c * 128:(c + 1) * 128]), 16)

        def emit_ffn(callidx):
            STOP = stop_after
            tag = f"_f{callidx}"
            with contextlib.ExitStack() as fctx:
                def fsb(nm, shape, dt):
                    return fctx.enter_context(nc.sbuf_tensor(nm + tag, shape, dt))
                actx = contextlib.ExitStack()

                def axb(nm, shape, dt):
                    return actx.enter_context(nc.sbuf_tensor(nm + tag, shape, dt))
                rctx = contextlib.ExitStack()

                def rsb(nm, shape, dt):
                    return rctx.enter_context(nc.sbuf_tensor(nm + tag, shape, dt))

                xn_fm = axb("xn_fm", [128, DC, TLOC], BF16)
                emit_ln_to_fm(xn_fm, tag)
                if callidx == 0:
                    tap_dram("xn", xn_dram.ap())
                if STOP == "ln":
                    rctx.close()
                    actx.close()
                    return

                sSt.wait(gp)
                sCc.wait(gp)
                sCc.inc(gp.collective_compute(
                    "AllGather", ALU.bypass, RG_ALL,
                    ins=[xn_dram.ap()], outs=[x_all.ap()]))

                sSt.wait(gp, zeros_done_v)
                for blk in range(NCORE):
                    sGd.inc(gp.dma_start(
                        out=partial.ap()[blk * TLOC:(blk + 1) * TLOC, :],
                        in_=zeros_dram.ap()), 16)

                scoresT = rsb("scoresT", [E, TLOC], F32)
                biasedT = rsb("biasedT", [E, TLOC], F32)
                selT = rsb("selT", [E, TLOC], F32)
                inclT = rsb("inclT", [E, TLOC], F32)
                gposT = rsb("gposT", [E, TLOC], F32)
                ones_scan = rsb("ones_scan", [E, TLOC], F32)
                btm = rsb("btm", [128, NT, E], F32)
                gtm = rsb("gtm", [128, NT, E], F32)
                top8 = rsb("top8", [128, NT, 8], F32)
                idxu = rsb("idxu", [128, 8], U32)
                idx8f = rsb("idx8f", [128, NT, 8], F32)
                stile = rsb("stile", [128, E], F32)
                junk32 = rsb("junk32", [128, E], F32)
                cnt_s = rsb("cnt_s", [NCORE, E], F32)
                prefix_s = rsb("prefix_s", [E, 1], F32)
                pairs_v = rsb("pairs_v", [128, NT, 8], F32)
                scat_i = rsb("scat_i", [128, NT, 4], I32)
                wk4 = rsb("wk4", [128, 4], F32)
                va4 = rsb("va4", [128, 4], F32)
                sc4 = rsb("sc4", [128, 4], F32)
                gak_t = rsb("gak_t", [128, 1], F32)
                bak_t = rsb("bak_t", [128, 1], F32)
                m4 = rsb("m4", [128, 1], F32)
                wsum = rsb("wsum", [128, 1], F32)
                winvh = rsb("winvh", [128, 1], F32)
                cnt_row = rsb("cnt_row", [1, E], F32)

                with nc.psum_tensor("ps_sc" + tag, [E, 512], F32) as ps_sc, \
                     nc.psum_tensor("ps_sc2" + tag, [E, 512], F32) as ps_sc2, \
                     nc.psum_tensor("ps_tr" + tag, [128, 512], F32) as ps_tr, \
                     nc.psum_tensor("ps_tr2" + tag, [128, 512], F32) as ps_tr2, \
                     nc.psum_tensor("ps_pref" + tag, [E, 512], F32) as ps_pref:

                    sXb.wait(pe)
                    sIn.wait(pe)
                    for g in range(4):
                        ps = ps_sc if g % 2 == 0 else ps_sc2
                        if g >= 2:
                            sSc.wait(pe, sSc.v - 1)
                        for kc in range(DC):
                            inst = pe.matmul(ps[:], lhsT=gw_s[:, kc, :],
                                             rhs=xn_fm[:, kc, g * 512:(g + 1) * 512],
                                             start=(kc == 0), stop=(kc == DC - 1))
                            if kc == DC - 1:
                                sPE.inc(inst)
                        sPE.wait(sc)
                        sSc.inc(sc.activation(out=scoresT[:, g * 512:(g + 1) * 512],
                                              in_=ps[:], func=ACT.Sigmoid,
                                              bias=0.0, scale=1.0))
                    sSc.wait(ve)
                    sV.inc(ve.tensor_scalar_add(biasedT[:], scoresT[:], bias_col[:]))
                    if callidx == 0:
                        tap_sbuf("scoresT", scoresT[:])
                    for j in range(NT):
                        pst = ps_tr if j % 2 == 0 else ps_tr2
                        sV.wait(pe)
                        sPE.inc(pe.transpose(pst[0:128, 0:E],
                                             in_=biasedT[:, j * 128:(j + 1) * 128],
                                             identity=identf[0:E, 0:E]))
                        sPE.wait(ve)
                        sV.inc(ve.tensor_copy(out=btm[:, j, :], in_=pst[0:128, 0:E]))
                    for j in range(NT):
                        pst = ps_tr if j % 2 == 0 else ps_tr2
                        sV.inc(ve.max(out=top8[:, j, :], in_=btm[:, j, :]))
                        sV.inc(ve.max_index(out=idxu[:], in_max=top8[:, j, :],
                                            in_values=btm[:, j, :]))
                        sV.inc(ve.tensor_copy(out=idx8f[:, j, :], in_=idxu[:]))
                        sV.inc(ve.tensor_scalar(stile[:], btm[:, j, :],
                                                top8[:, j, 3:4], None, op0=ALU.is_ge))
                        sV.wait(pe)
                        sPE.inc(pe.transpose(pst[0:E, 0:128], in_=stile[:],
                                             identity=identf[:]))
                        sPE.wait(ve)
                        sV.inc(ve.tensor_copy(out=selT[:, j * 128:(j + 1) * 128],
                                              in_=pst[0:E, 0:128]))
                    if callidx == 0:
                        tap_sbuf("top8", top8[:].rearrange("p j k -> p (j k)"))
                        tap_sbuf("idx8", idx8f[:].rearrange("p j k -> p (j k)"))
                        tap_sbuf("selT", selT[:])
                    sV.inc(ve.memset(ones_scan[:], 1.0))
                    sV.wait(nc.vector)
                    sV.inc(ve.tensor_tensor_scan(out=inclT[:], data0=selT[:],
                                                 data1=ones_scan[:], initial=0.0,
                                                 op0=ALU.add, op1=ALU.mult))
                    sV.inc(ve.tensor_sub(gposT[:], inclT[:], selT[:]))
                    if callidx == 0:
                        tap_sbuf("incl", inclT[:])
                        tap_sbuf("ones", ones_scan[:])
                    sV.wait(pe)
                    sPE.inc(pe.transpose(ps_tr[0:1, 0:E],
                                         in_=inclT[:, TLOC - 1:TLOC],
                                         identity=identf[0:E, 0:E]))
                    sPE.wait(ve)
                    sV.inc(ve.tensor_copy(out=cnt_row[:], in_=ps_tr[0:1, 0:E]))
                    sV.wait(sy)
                    sSt.inc(sy.dma_start(
                        out=cnt_loc.ap().rearrange("(one e) -> one e", one=1),
                        in_=cnt_row[:]), 16)
                    sSt.wait(gp)
                    sCc.inc(gp.collective_compute("AllGather", ALU.bypass, RG_ALL,
                                                  ins=[cnt_loc.ap()],
                                                  outs=[cnt_all.ap()]))
                    sCc.wait(sy)
                    sIn.inc(sy.dma_start(out=cnt_s[:], in_=cnt_all.ap()), 16)
                    sIn.wait(pe)
                    sPE.inc(pe.matmul(ps_pref[:, 0:1], lhsT=cnt_s[:], rhs=premask_s[:],
                                      start=True, stop=True))
                    sPE.wait(ve)
                    sV.inc(ve.tensor_copy(out=prefix_s[:], in_=ps_pref[:, 0:1]))
                    if callidx == 0:
                        tap_sbuf("cnta", cnt_s[:])
                        tap_sbuf("pref", prefix_s[:])
                    sV.inc(ve.tensor_scalar_add(gposT[:], gposT[:], prefix_s[:]))
                    if callidx == 0:
                        tap_sbuf("gposT", gposT[:])
                    for j in range(NT):
                        pst = ps_tr if j % 2 == 0 else ps_tr2
                        sV.wait(pe)
                        sPE.inc(pe.transpose(pst[0:128, 0:E],
                                             in_=gposT[:, j * 128:(j + 1) * 128],
                                             identity=identf[0:E, 0:E]))
                        sPE.wait(ve)
                        sV.inc(ve.tensor_copy(out=gtm[:, j, :], in_=pst[0:128, 0:E]))

                for j in range(NT):
                    for k in range(4):
                        sV.inc(ve.scalar_tensor_tensor(
                            out=junk32[:], in0=iota32[:],
                            scalar=idx8f[:, j, k:k + 1], in1=gtm[:, j, :],
                            op0=ALU.is_equal, op1=ALU.mult, accum_out=gak_t[:]))
                        sV.inc(ve.scalar_tensor_tensor(
                            out=junk32[:], in0=iota32[:],
                            scalar=idx8f[:, j, k:k + 1], in1=bias_bc[:],
                            op0=ALU.is_equal, op1=ALU.mult, accum_out=bak_t[:]))
                        sV.inc(ve.tensor_sub(wk4[:, k:k + 1], top8[:, j, k:k + 1],
                                             bak_t[:]))
                        sV.inc(ve.tensor_scalar(va4[:, k:k + 1], gak_t[:],
                                                float(CAP), None, op0=ALU.is_lt))
                        sV.inc(ve.tensor_scalar_min(m4[:], gak_t[:], float(CAP)))
                        sV.inc(ve.scalar_tensor_tensor(
                            out=sc4[:, k:k + 1], in0=idx8f[:, j, k:k + 1],
                            scalar=float(CAP + 1), in1=m4[:],
                            op0=ALU.mult, op1=ALU.add))
                    sV.inc(ve.reduce_sum(out=wsum[:], in_=wk4[:], axis=AX))
                    sV.inc(ve.tensor_scalar_max(wsum[:], wsum[:], 1e-12))
                    sV.wait(sc)
                    sSc.inc(sc.activation(out=m4[:], in_=wsum[:], func=ACT.Ln,
                                          bias=0.0, scale=1.0))
                    sSc.inc(sc.activation(out=winvh[:], in_=m4[:], func=ACT.Exp,
                                          bias=0.0, scale=-1.0))
                    sSc.wait(ve)
                    sV.inc(ve.tensor_scalar_mul(winvh[:], winvh[:], 0.5))
                    for k in range(4):
                        sV.inc(ve.tensor_mul(wk4[:, k:k + 1], wk4[:, k:k + 1],
                                             winvh[:]))
                        sV.inc(ve.tensor_mul(wk4[:, k:k + 1], wk4[:, k:k + 1],
                                             va4[:, k:k + 1]))
                        sV.inc(ve.tensor_copy(out=pairs_v[:, j, 2 * k:2 * k + 1],
                                              in_=tokids[:, j:j + 1]))
                        sV.inc(ve.tensor_copy(out=pairs_v[:, j, 2 * k + 1:2 * k + 2],
                                              in_=wk4[:, k:k + 1]))
                    sV.inc(ve.tensor_copy(out=scat_i[:, j, :], in_=sc4[:]))
                if callidx == 0:
                    tap_sbuf("gates",
                             pairs_v[:].rearrange("p j (k two) -> p (j k) two",
                                                  two=2)[:, :, 1])
                spflat = sp_loc.ap().rearrange("r two -> (r two)")
                n_sp = E * (CAP + 1) * 2
                sCc.wait(gp)
                sGd.inc(gp.dma_start(
                    out=spflat[0:65536].rearrange("(p f) -> p f", p=128),
                    in_=neg1t[:, 0:512]), 16)
                sGd.inc(gp.dma_start(
                    out=spflat[65536:n_sp].rearrange("(one f) -> one f", one=1),
                    in_=neg1t[0:1, 0:n_sp - 65536]), 16)
                sGd.wait(gp)
                sV.wait(gp)
                for j in range(NT):
                    for k in range(4):
                        sGi.inc(gp.indirect_dma_start(
                            out=sp_loc.ap(),
                            out_offset=IndirectOffsetOnAxis(
                                ap=scat_i[:, j, k:k + 1], axis=0),
                            in_=pairs_v[:, j, 2 * k:2 * k + 2],
                            in_offset=None), 16)
                sGi.wait(gp)
                sCc.inc(gp.collective_compute("AllReduce", ALU.max, RG_ALL,
                                              ins=[sp_loc.ap()],
                                              outs=[sp_all.ap()]))
                if callidx == 0:
                    tap_dram("spall", sp_all.ap())
                if STOP == "routing":
                    rctx.close()
                    actx.close()
                    return

                sGi.wait(ve)
                sGi.wait(sc)
                rctx.close()
                sctx = contextlib.ExitStack()
                hsh = sctx.enter_context(nc.sbuf_tensor("hsh" + tag, [128, 4, TLOC], BF16))
                hs_st = sctx.enter_context(nc.sbuf_tensor("hs_st" + tag, [128, 2, 512], BF16))
                with nc.psum_tensor("ps_h1" + tag, [128, 512], F32) as ps_h1, \
                     nc.psum_tensor("ps_h3" + tag, [128, 512], F32) as ps_h3, \
                     nc.psum_tensor("ps_h1b" + tag, [128, 512], F32) as ps_h1b, \
                     nc.psum_tensor("ps_h3b" + tag, [128, 512], F32) as ps_h3b:
                    gi = 0
                    vwait_sh = {}
                    for mc in range(4):
                        for ng in range(4):
                            par = gi % 2
                            p1 = ps_h1 if par == 0 else ps_h1b
                            p3 = ps_h3 if par == 0 else ps_h3b
                            if gi >= 2:
                                sV.wait(pe, vwait_sh[gi - 2])
                                sV.wait(sc, vwait_sh[gi - 2])
                            for kc in range(DC):
                                pe.matmul(p1[:],
                                          lhsT=sw1_s[:, kc, mc * 128:(mc + 1) * 128],
                                          rhs=xn_fm[:, kc, ng * 512:(ng + 1) * 512],
                                          start=(kc == 0), stop=(kc == DC - 1))
                                inst = pe.matmul(
                                    p3[:], lhsT=sw3_s[:, kc, mc * 128:(mc + 1) * 128],
                                    rhs=xn_fm[:, kc, ng * 512:(ng + 1) * 512],
                                    start=(kc == 0), stop=(kc == DC - 1))
                                if kc == DC - 1:
                                    sPE.inc(inst)
                            sPE.wait(sc)
                            sSc.inc(sc.activation(out=hs_st[:, par, :], in_=p1[:],
                                                  func=ACT.Silu, bias=0.0, scale=1.0))
                            sSc.wait(ve)
                            sV.inc(ve.tensor_mul(hsh[:, mc, ng * 512:(ng + 1) * 512],
                                                 hs_st[:, par, :], p3[:]))
                            vwait_sh[gi] = sV.v
                            gi += 1
                    gi = 0
                    vwait2 = {}
                    for j in range(NT):
                        for dh in range(2):
                            par = gi % 2
                            p1 = ps_h1 if par == 0 else ps_h1b
                            if gi >= 2:
                                sV.wait(pe, vwait2[gi - 2])
                            for ic in range(4):
                                inst = pe.matmul(
                                    p1[:], lhsT=hsh[:, ic, j * 128:(j + 1) * 128],
                                    rhs=sw2_s[:, ic, dh * 512:(dh + 1) * 512],
                                    start=(ic == 0), stop=(ic == 3))
                                if ic == 3:
                                    sPE.inc(inst)
                            sPE.wait(ve)
                            sV.inc(ve.scalar_tensor_tensor(
                                out=x_res[:, j, dh * 512:(dh + 1) * 512],
                                in0=p1[:], scalar=0.5,
                                in1=x_res[:, j, dh * 512:(dh + 1) * 512],
                                op0=ALU.mult, op1=ALU.add))
                            vwait2[gi] = sV.v
                            gi += 1

                if STOP == "shared":
                    sctx.close()
                    actx.close()
                    return
                sPE.wait(ve)
                sPE.wait(gp)
                sctx.close()
                actx.close()
                pairs_t = fsb("pairs_t", [128, 2 * SPP], F32)
                ids_f = fsb("ids_f", [128, SPP], F32)
                gts = fsb("gts", [128, SPP], F32)
                mneg = fsb("mneg", [128, SPP], U32)
                idx_i = fsb("idx_i", [128, SPP], I32)
                peidx = fsb("peidx", [128, 1], I32)
                peidx_f = fsb("peidx_f", [128, 1], F32)
                xsta = fsb("xsta", [128, 2, D], BF16)
                xe_fm = fsb("xe_fm", [128, DC, CAP], BF16)
                wb1 = fsb("wb1", [128, DC, IFF], BF16)
                wb3 = fsb("wb3", [128, DC, IFF], BF16)
                wb2 = fsb("wb2", [128, 4, D], BF16)
                h_fm = fsb("h_fm", [128, 4, CAP], BF16)
                hes = fsb("hes", [128, 2, 512], BF16)
                yesta = fsb("yesta", [128, 2, D], BF16)

                sCc.wait(gp)
                sCc.wait(sy)
                sCc.wait(ve)
                prev_scatter_v = sGi.v
                with nc.psum_tensor("pe_h1" + tag, [128, 512], F32) as pp_h1, \
                     nc.psum_tensor("pe_h3" + tag, [128, 512], F32) as pp_h3, \
                     nc.psum_tensor("pe_h1b" + tag, [128, 512], F32) as pp_h1b, \
                     nc.psum_tensor("pe_h3b" + tag, [128, 512], F32) as pp_h3b, \
                     nc.psum_tensor("pe_y1" + tag, [128, 512], F32) as pp_y1, \
                     nc.psum_tensor("pe_y2" + tag, [128, 512], F32) as pp_y2, \
                     nc.psum_tensor("pe_y1b" + tag, [128, 512], F32) as pp_y1b, \
                     nc.psum_tensor("pe_y2b" + tag, [128, 512], F32) as pp_y2b:
                    for el in range(EL):
                        sV.inc(ve.scalar_tensor_tensor(
                            out=peidx_f[:], in0=eids_s[:, el:el + 1],
                            scalar=float(CAP + 1), in1=p32t[:],
                            op0=ALU.mult, op1=ALU.add))
                        sV.inc(ve.tensor_copy(out=peidx[:], in_=peidx_f[:]))
                        sV.wait(gp)
                        sGi.inc(gp.indirect_dma_start(
                            out=pairs_t[:], out_offset=None, in_=sp_all.ap(),
                            in_offset=IndirectOffsetOnAxis(ap=peidx[:], axis=0)), 16)
                        sGi.wait(ve)
                        pr = pairs_t[:].rearrange("p (s two) -> p s two", two=2)
                        sV.inc(ve.tensor_copy(out=ids_f[:], in_=pr[:, :, 0]))
                        sV.inc(ve.tensor_copy(out=gts[:], in_=pr[:, :, 1]))
                        sV.inc(ve.tensor_scalar(mneg[:], ids_f[:], 0.0, None,
                                                op0=ALU.is_lt))
                        sV.inc(ve.copy_predicated(ids_f[:], mneg[:], zf32[:, 0:SPP]))
                        sV.inc(ve.copy_predicated(gts[:], mneg[:], zf32[:, 0:SPP]))
                        sV.inc(ve.tensor_copy(out=idx_i[:], in_=ids_f[:]))
                        sV.wait(gp)
                        sXb.wait(sy)
                        store_v = {}
                        for q in range(SPP):
                            par = q % 2
                            if q >= 2:
                                sSt.wait(gp, store_v[q - 2])
                            sGi.inc(gp.indirect_dma_start(
                                out=xsta[:, par, :], out_offset=None,
                                in_=x_all.ap(),
                                in_offset=IndirectOffsetOnAxis(
                                    ap=idx_i[:, q:q + 1], axis=0)), 16)
                            sGi.wait(sy)
                            sSt.inc(sy.dma_start(
                                out=xe_dram.ap()[q * 128:(q + 1) * 128, :],
                                in_=xsta[:, par, :]), 16)
                            store_v[q] = sSt.v
                        sPE.wait(sy)
                        sIn.inc(sy.dma_start(out=wb1[:], in_=w1_in[el].rearrange(
                            "(c p) i -> p c i", p=128)), 16)
                        sIn.inc(sy.dma_start(out=wb3[:], in_=w3_in[el].rearrange(
                            "(c p) i -> p c i", p=128)), 16)
                        sIn.inc(sy.dma_start(out=wb2[:], in_=w2_in[el].rearrange(
                            "(c p) d -> p c d", p=128)), 16)
                        if True:
                            sSt.wait(sy)
                            sPE.wait(sy)
                            for c in range(DC):
                                sXb.inc(sy.dma_start_transpose(
                                    out=xe_fm[:, c, :],
                                    in_=xe_dram.ap()[0:CAP,
                                                     c * 128:(c + 1) * 128]), 16)
                            sXb.wait(pe)
                            sIn.wait(pe)
                            gi = 0
                            vwaith = {}
                            for mc in range(4):
                                for ng in range(CAP // 512):
                                    par = gi % 2
                                    p1 = pp_h1 if par == 0 else pp_h1b
                                    p3 = pp_h3 if par == 0 else pp_h3b
                                    if gi >= 2:
                                        sV.wait(pe, vwaith[gi - 2])
                                        sV.wait(sc, vwaith[gi - 2])
                                    for kc in range(DC):
                                        pe.matmul(
                                            p1[:],
                                            lhsT=wb1[:, kc, mc * 128:(mc + 1) * 128],
                                            rhs=xe_fm[:, kc, ng * 512:(ng + 1) * 512],
                                            start=(kc == 0), stop=(kc == DC - 1))
                                        inst = pe.matmul(
                                            p3[:],
                                            lhsT=wb3[:, kc, mc * 128:(mc + 1) * 128],
                                            rhs=xe_fm[:, kc, ng * 512:(ng + 1) * 512],
                                            start=(kc == 0), stop=(kc == DC - 1))
                                        if kc == DC - 1:
                                            sPE.inc(inst)
                                    sPE.wait(sc)
                                    sSc.inc(sc.activation(out=hes[:, par, :],
                                                          in_=p1[:], func=ACT.Silu,
                                                          bias=0.0, scale=1.0))
                                    sSc.wait(ve)
                                    sV.inc(ve.tensor_mul(
                                        h_fm[:, mc, ng * 512:(ng + 1) * 512],
                                        hes[:, par, :], p3[:]))
                                    vwaith[gi] = sV.v
                                    gi += 1
                            gi = 0
                            evw = {}
                            giw = {}
                            for st in range(SPP):
                                q = st
                                par = gi % 2
                                py1 = pp_y1 if par == 0 else pp_y1b
                                py2 = pp_y2 if par == 0 else pp_y2b
                                if gi >= 2:
                                    sV.wait(pe, evw[gi - 2][0])
                                    sSc.wait(pe, evw[gi - 2][1])
                                for ic in range(4):
                                    i1 = pe.matmul(
                                        py1[:], lhsT=h_fm[:, ic, st * 128:(st + 1) * 128],
                                        rhs=wb2[:, ic, 0:512],
                                        start=(ic == 0), stop=(ic == 3))
                                    i2 = pe.matmul(
                                        py2[:], lhsT=h_fm[:, ic, st * 128:(st + 1) * 128],
                                        rhs=wb2[:, ic, 512:1024],
                                        start=(ic == 0), stop=(ic == 3))
                                    if ic == 3:
                                        sPE.inc(i1)
                                        sPE.inc(i2)
                                if gi >= 2:
                                    sGi.wait(ve, giw[gi - 2])
                                    sGi.wait(sc, giw[gi - 2])
                                sPE.wait(ve, sPE.v - 1)
                                sV.inc(ve.tensor_scalar_mul(yesta[:, par, 0:512],
                                                            py1[:], gts[:, q:q + 1]))
                                sPE.wait(sc)
                                sSc.inc(sc.activation(out=yesta[:, par, 512:1024],
                                                      in_=py2[:], func=ACT.Identity,
                                                      bias=0.0, scale=gts[:, q:q + 1]))
                                evw[gi] = (sV.v, sSc.v)
                                sV.wait(gp)
                                sSc.wait(gp)
                                if el > 0 and gi == 0:
                                    sGi.wait(gp, prev_scatter_v)
                                sGi.inc(gp.indirect_dma_start(
                                    out=partial.ap(),
                                    out_offset=IndirectOffsetOnAxis(
                                        ap=idx_i[:, q:q + 1], axis=0),
                                    in_=yesta[:, par, :], in_offset=None,
                                    compute_op=ALU.add), 16)
                                giw[gi] = sGi.v
                                gi += 1
                        prev_scatter_v = sGi.v
                if STOP == "experts":
                    return
                sGi.wait(gp)
                sGd.wait(gp)
                sCc.inc(gp.collective_compute(
                    "ReduceScatter", ALU.add, RG_ALL,
                    ins=[partial.ap()], outs=[moe_dram.ap()]))
                if callidx == 0:
                    tap_dram("moe", moe_dram.ap())
                msta = fsb("msta", [128, 2, D], BF16)
                sCc.wait(sy)
                mvw = {}
                for j in range(NT):
                    par = j % 2
                    if j >= 2:
                        sV.wait(sy, mvw[j - 2])
                    sIn.inc(sy.dma_start(out=msta[:, par, :],
                                         in_=moe_dram.ap()[j * 128:(j + 1) * 128, :]),
                            16)
                    sIn.wait(ve)
                    sV.inc(ve.tensor_add(x_res[:, j, :], x_res[:, j, :],
                                         msta[:, par, :]))
                    mvw[j] = sV.v

        def emit_attn():
            tag = "_a"
            with contextlib.ExitStack() as actx:
                def asb(nm, shape, dt):
                    return actx.enter_context(nc.sbuf_tensor(nm + tag, shape, dt))

                lctx = contextlib.ExitStack()

                def lsb2(nm, shape, dt):
                    return lctx.enter_context(nc.sbuf_tensor(nm + tag, shape, dt))

                xn_fm = lsb2("xn_fm", [128, DC, TLOC], BF16)
                emit_ln_to_fm(xn_fm, tag)

                lat_loc = lsb2("lat_loc", [LR, TLOC], BF16)
                wqst = lsb2("wqst", [128, DC, 128], BF16)
                qst = lsb2("qst", [128, 2, 512], BF16)

                with nc.psum_tensor("pa_l1", [LR, 512], F32) as pa_l1, \
                     nc.psum_tensor("pa_l2", [LR, 512], F32) as pa_l2, \
                     nc.psum_tensor("pa_q", [128, 512], F32) as pa_q, \
                     nc.psum_tensor("pa_q2", [128, 512], F32) as pa_q2:
                    sXb.wait(pe)
                    vw = {}
                    for ng in range(4):
                        ps = pa_l1 if ng % 2 == 0 else pa_l2
                        if ng >= 2:
                            sV.wait(pe, vw[ng - 2])
                        for kc in range(DC):
                            inst = pe.matmul(ps[:], lhsT=wkvd_s[:, kc, :],
                                             rhs=xn_fm[:, kc, ng * 512:(ng + 1) * 512],
                                             start=(kc == 0), stop=(kc == DC - 1))
                            if kc == DC - 1:
                                sPE.inc(inst)
                        sPE.wait(ve)
                        sV.inc(ve.tensor_copy(out=lat_loc[:, ng * 512:(ng + 1) * 512],
                                              in_=ps[:]))
                        vw[ng] = sV.v
                    sV.wait(sy)
                    sSt.inc(sy.dma_start(out=lat_in_d.ap(), in_=lat_loc[:]), 16)
                    sSt.wait(gp)
                    sCc.inc(gp.collective_compute(
                        "AllGather", ALU.bypass, RG_BATCH,
                        ins=[lat_in_d.ap()], outs=[lat_all_d.ap()]))
                    tap_dram("lat", lat_all_d.ap())
                    qgi = 0
                    qsw = {}
                    for h in range(NH):
                        for m in range(2):
                            col = h * HD + m * 128
                            sPE.wait(sy)
                            for kc in range(DC):
                                sIn.inc(sy.dma_start(
                                    out=wqst[:, kc, :],
                                    in_=wq_bf.ap()[kc * 128:(kc + 1) * 128,
                                                   col:col + 128]), 16)
                            sIn.wait(pe)
                            for ng in range(4):
                                par = qgi % 2
                                ps = pa_q if par == 0 else pa_q2
                                if qgi >= 2:
                                    sSt.wait(ve, qsw[qgi - 2])
                                    sV.wait(pe, qvw[qgi - 2])
                                for kc in range(DC):
                                    inst = pe.matmul(
                                        ps[:], lhsT=wqst[:, kc, :],
                                        rhs=xn_fm[:, kc, ng * 512:(ng + 1) * 512],
                                        start=(kc == 0), stop=(kc == DC - 1))
                                    if kc == DC - 1:
                                        sPE.inc(inst)
                                sPE.wait(ve)
                                sV.inc(ve.tensor_copy(out=qst[:, par, :], in_=ps[:]))
                                sV.wait(sy)
                                sSt.inc(sy.dma_start(
                                    out=q_dram.ap()[col:col + 128,
                                                    ng * 512:(ng + 1) * 512],
                                    in_=qst[:, par, :]), 16)
                                if qgi == 0:
                                    qvw = {}
                                qvw[qgi] = sV.v
                                qsw[qgi] = sSt.v
                                qgi += 1
                    sPE.wait(ve)
                    sPE.wait(sy)
                lctx.close()

                lat_fm = asb("lat_fm", [LR, 4 * TLOC], BF16)
                q_fm = asb("q_fm", [128, 2, TLOC], BF16)
                k_fm = asb("k_fm", [128, 2, 4 * TLOC], BF16)
                v_tm = asb("v_tm", [128, 64, 258], BF16)
                pT = asb("pT", [128, 2, 512], BF16)
                dcl = asb("dcl", [128, 1], F32)
                dinv = asb("dinv", [128, 1], F32)
                dscr = asb("dscr", [128, 1], F32)
                ao_st = asb("ao_st", [128, 4, HD], BF16)

                sCc.wait(sy)
                for r in range(4):
                    sIn.inc(sy.dma_start(
                        out=lat_fm[:, r * TLOC:(r + 1) * TLOC],
                        in_=lat_all_d.ap()[r * LR:(r + 1) * LR, :]), 16)

                with nc.psum_tensor("pa_k", [128, 512], F32) as pa_k, \
                     nc.psum_tensor("pa_k2", [128, 512], F32) as pa_k2, \
                     nc.psum_tensor("pa_st", [128, 512], F32) as pa_st, \
                     nc.psum_tensor("pa_st2", [128, 512], F32) as pa_st2, \
                     nc.psum_tensor("pa_at0", [128, 512], F32) as pa_at0, \
                     nc.psum_tensor("pa_at1", [128, 512], F32) as pa_at1, \
                     nc.psum_tensor("pa_at2", [128, 512], F32) as pa_at2, \
                     nc.psum_tensor("pa_at3", [128, 512], F32) as pa_at3:
                    pa_at = [pa_at0, pa_at1, pa_at2, pa_at3]
                    ao_sto = {}
                    for h in range(NH):
                        sPE.wait(sy)
                        for m in range(2):
                            col = h * HD + m * 128
                            sIn.inc(sy.dma_start(
                                out=q_fm[:, m, :],
                                in_=q_dram.ap()[col:col + 128, :]), 16)
                        sIn.wait(pe)
                        sPE.wait(ve)
                        vw = {}
                        gi = 0
                        for m in range(2):
                            col = h * HD + m * 128
                            for ng in range(16):
                                ps = pa_k if gi % 2 == 0 else pa_k2
                                if gi >= 2:
                                    sV.wait(pe, vw[gi - 2])
                                sPE.inc(pe.matmul(
                                    ps[:], lhsT=wkup_s[:, col:col + 128],
                                    rhs=lat_fm[:, ng * 512:(ng + 1) * 512],
                                    start=True, stop=True))
                                sPE.wait(ve)
                                sV.inc(ve.tensor_copy(
                                    out=k_fm[:, m, ng * 512:(ng + 1) * 512],
                                    in_=ps[:]))
                                vw[gi] = sV.v
                                gi += 1
                        sPE.wait(sc)
                        sPE.wait(ve)
                        sSc.wait(ve)
                        sV.inc(ve.memset(v_tm[:, :, 256:257], 1.0))
                        sV.wait(pe)
                        scw = {}
                        for kc in range(64):
                            ps = pa_k if kc % 2 == 0 else pa_k2
                            if kc >= 2:
                                sSc.wait(pe, scw[kc - 2])
                            sPE.inc(pe.matmul(ps[0:128, 0:HD],
                                              lhsT=lat_fm[:, kc * 128:(kc + 1) * 128],
                                              rhs=wvup_s[:, h * HD:(h + 1) * HD],
                                              start=True, stop=True))
                            sPE.wait(sc)
                            sSc.inc(sc.activation(out=v_tm[:, kc, 0:HD],
                                                  in_=ps[0:128, 0:HD],
                                                  func=ACT.Identity,
                                                  bias=0.0, scale=1.0))
                            scw[kc] = sSc.v
                        for g in range(4):
                            expw = {}
                            avw = {}
                            for kc in range(64):
                                par = kc % 2
                                ps = pa_st if par == 0 else pa_st2
                                for dkc in range(2):
                                    inst = pe.matmul(
                                        ps[:],
                                        lhsT=k_fm[:, dkc, kc * 128:(kc + 1) * 128],
                                        rhs=q_fm[:, dkc, g * 512:(g + 1) * 512],
                                        start=(dkc == 0), stop=(dkc == 1))
                                    if dkc == 1:
                                        sPE.inc(inst)
                                if kc >= 2:
                                    sPE.wait(sc, avw[kc - 2])
                                sPE.wait(sc)
                                sSc.inc(sc.activation(out=pT[:, par, :], in_=ps[:],
                                                      func=ACT.Exp, bias=m30[:],
                                                      scale=1.0 / 16.0))
                                expw[kc] = sSc.v
                                sSc.wait(pe)
                                for sub in range(4):
                                    inst = pe.matmul(
                                        pa_at[sub][:, 0:257],
                                        lhsT=pT[:, par, sub * 128:(sub + 1) * 128],
                                        rhs=v_tm[:, kc, 0:257],
                                        start=(kc == 0), stop=(kc == 63))
                                    if sub == 3:
                                        sPE.inc(inst)
                                avw[kc] = sPE.v
                                if kc >= 2:
                                    sSc.wait(pe, expw[kc - 1])
                            sPE.wait(ve)
                            if (h, g) != (0, 0):
                                sSt.wait(ve, ao_sto[(h, g)])
                            for sub in range(4):
                                sV.inc(ve.tensor_scalar_max(dcl[:],
                                                            pa_at[sub][:, 256:257],
                                                            1e-30))
                                sV.wait(sc)
                                sSc.inc(sc.activation(out=dscr[:], in_=dcl[:],
                                                      func=ACT.Ln, bias=0.0, scale=1.0))
                                sSc.inc(sc.activation(out=dinv[:], in_=dscr[:],
                                                      func=ACT.Exp, bias=0.0,
                                                      scale=-1.0))
                                sSc.wait(ve)
                                sV.inc(ve.tensor_scalar_mul(ao_st[:, sub, :],
                                                            pa_at[sub][:, 0:HD],
                                                            dinv[:]))
                                sV.wait(sy)
                                row = g * 512 + sub * 128
                                sSt.inc(sy.dma_start(
                                    out=ao_dram.ap()[row:row + 128,
                                                     h * HD:(h + 1) * HD],
                                    in_=ao_st[:, sub, :]), 16)
                            nh2, ng2 = (h, g + 1) if g < 3 else (h + 1, 0)
                            ao_sto[(nh2, ng2)] = sSt.v
                            sSt.wait(pe)
                aofm = asb("aofm", [128, 2, 512], BF16)
                wost = asb("wost", [128, 2, D], BF16)
                with nc.psum_tensor("pw0", [128, 512], F32) as pw0, \
                     nc.psum_tensor("pw1", [128, 512], F32) as pw1, \
                     nc.psum_tensor("pw2", [128, 512], F32) as pw2, \
                     nc.psum_tensor("pw3", [128, 512], F32) as pw3, \
                     nc.psum_tensor("pw4", [128, 512], F32) as pw4, \
                     nc.psum_tensor("pw5", [128, 512], F32) as pw5, \
                     nc.psum_tensor("pw6", [128, 512], F32) as pw6, \
                     nc.psum_tensor("pw7", [128, 512], F32) as pw7:
                    pw = [pw0, pw1, pw2, pw3, pw4, pw5, pw6, pw7]
                    for quad in range(4):
                        if quad > 0:
                            sV.wait(pe)
                        pew = {}
                        for fc in range(16):
                            par = fc % 2
                            if fc >= 2:
                                sPE.wait(sy, pew[fc - 2])
                            sXb.inc(sy.dma_start_transpose(
                                out=aofm[:, par, :],
                                in_=ao_dram.ap()[quad * 512:(quad + 1) * 512,
                                                 fc * 128:(fc + 1) * 128]), 16)
                            sIn.inc(sy.dma_start(
                                out=wost[:, par, :],
                                in_=woT_bf.ap()[fc * 128:(fc + 1) * 128, :]), 16)
                            sXb.wait(pe)
                            sIn.wait(pe)
                            for tj in range(4):
                                for dh in range(2):
                                    inst = pe.matmul(
                                        pw[tj * 2 + dh][:],
                                        lhsT=aofm[:, par, tj * 128:(tj + 1) * 128],
                                        rhs=wost[:, par, dh * 512:(dh + 1) * 512],
                                        start=(fc == 0), stop=(fc == 15))
                                    if tj == 3 and dh == 1:
                                        sPE.inc(inst)
                            pew[fc] = sPE.v
                        sPE.wait(ve)
                        for tj in range(4):
                            j = quad * 4 + tj
                            for dh in range(2):
                                sV.inc(ve.tensor_add(
                                    x_res[:, j, dh * 512:(dh + 1) * 512],
                                    x_res[:, j, dh * 512:(dh + 1) * 512],
                                    pw[tj * 2 + dh][:]))

        emit_ffn(0)
        if stop_after == "ln":
            pass
        if "xres1" in taps:
            sV.wait(sy)
            sSt.inc(sy.dma_start(
                out=taps["xres1"].rearrange("(j p) d -> p j d", p=128),
                in_=x_res[:]), 16)
        if do_attn:
            emit_attn()
        if "xres2" in taps:
            sV.wait(sy)
            sSt.inc(sy.dma_start(
                out=taps["xres2"].rearrange("(j p) d -> p j d", p=128),
                in_=x_res[:]), 16)
        if n_ffn >= 2:
            emit_ffn(1)

        sV.wait(sy)
        sSc.wait(sy)
        sSt.inc(sy.dma_start(out=out_t.rearrange("(j p) d -> p j d", p=128),
                             in_=x_res[:]), 16)
        sSt.wait(sy)
        sGi.wait(gp)
        sGd.wait(gp)
        sCc.wait(gp)
    return nc


def make_inputs(g, core):
    T = np.ascontiguousarray
    x = np.asarray(g["L0_attn_x"]).reshape(NCORE * TLOC, D)
    return {
        "x": T(x[core * TLOC:(core + 1) * TLOC]),
        "wqT": T(np.asarray(g["L0_attn_Wq"]).T),
        "wkvdT": T(np.asarray(g["L0_attn_Wkv_down"]).T),
        "wkupT": T(np.asarray(g["L0_attn_Wk_up"]).T),
        "wvupT": T(np.asarray(g["L0_attn_Wv_up"]).T),
        "woT": T(np.asarray(g["L0_attn_Wo"]).T),
        "gwT": T(np.asarray(g["L0_ffn_gate_w"]).T),
        "bias": T(np.asarray(g["L0_ffn_bias"])),
        "w1": T(np.asarray(g["L0_ffn_W1"])[core * EL:(core + 1) * EL]),
        "w3": T(np.asarray(g["L0_ffn_W3"])[core * EL:(core + 1) * EL]),
        "w2": T(np.asarray(g["L0_ffn_W2"])[core * EL:(core + 1) * EL]),
        "sw1T": T(np.asarray(g["L0_ffn_sw1"]).T),
        "sw3T": T(np.asarray(g["L0_ffn_sw3"]).T),
        "sw2T": T(np.asarray(g["L0_ffn_sw2"]).T),
        "eids": np.tile(np.arange(core * EL, (core + 1) * EL, dtype=np.float32),
                        (128, 1)),
        "tokoff": np.full((128, 1), core * TLOC, np.float32),
        "premask": (np.arange(NCORE) < core).astype(np.float32).reshape(NCORE, 1),
    }


def kernel(**inputs):
    nc = build()
    in_maps = [make_inputs(inputs, c) for c in range(NCORE)]
    res = run_bass_kernel_spmd(nc, in_maps, core_ids=list(range(NCORE)))
    out = np.concatenate([res.results[c]["out"] for c in range(NCORE)], axis=0)
    return out.reshape(2, NCORE * TLOC // 2, D)
